# version 08
# baseline (speedup 1.0000x reference)
import contextlib

import numpy as np
import ml_dtypes

import concourse.bass as bass
import concourse.mybir as mybir
from concourse.bass_utils import run_bass_kernel_spmd

F32 = mybir.dt.float32
BF16 = mybir.dt.bfloat16
AF = mybir.ActivationFunctionType
ALU = mybir.AluOpType

D = 2048
KT = 16
NT = 18
OUT_T = 16
TOK = NT * 128
OUT_TOK = OUT_T * 128
NH = 16
SHIFT = 12.0
SCALE = 128 ** -0.5
EPS = 1e-5

BUILD_DEBUG = False

STREAMS = ["pe", "act", "dve", "pool", "sync"]
STREAM_SEMS = {"pe": "pe", "act": "act", "dve": "dve", "pool": "pool",
               "sync": "sdma"}
DMA_SEMS = {"sync": "sdma", "pool": "gdma"}
ALL_SEMS = ["pe", "act", "dve", "pool", "sdma", "gdma"]


class Sched:

    def __init__(self):
        self.q = {s: [] for s in STREAMS}
        self.ctr = {s: 0 for s in ALL_SEMS}
        self.reg = {}

    def op(self, stream, fn, reads=(), writes=(), dma=False):
        sem = DMA_SEMS[stream] if dma else STREAM_SEMS[stream]
        waits = {}

        def merge(ev):
            if ev is not None and waits.get(ev[0], 0) < ev[1]:
                waits[ev[0]] = ev[1]

        for r in reads:
            rg = self.reg.get(r)
            if rg:
                merge(rg[0])
        for w in writes:
            rg = self.reg.get(w)
            if rg:
                merge(rg[0])
                for ev in rg[1]:
                    merge(ev)
        n = 16 if dma else 1
        self.ctr[sem] += n
        ev = (sem, self.ctr[sem])
        for r in reads:
            self.reg.setdefault(r, [None, []])[1].append(ev)
        for w in writes:
            self.reg[w] = [ev, []]
        self.q[stream].append((waits, fn, sem, n))

    def barrier(self):
        snap = dict(self.ctr)
        for s in self.q:
            self.q[s].append((snap, None, None, 0))
        self.reg = {}


def _drained(fn):
    def run(e):
        fn(e)
        return e.drain()
    return run


def _build():
    nc = bass.Bass("TRN2", target_bir_lowering=False)

    def din(name, shape, dt=F32):
        return nc.dram_tensor(name, shape, dt, kind="ExternalInput").ap()

    x0c = din("x0c", [D, TOK])
    wg_bl = din("wg_bl", [64, 128, KT * 128])
    wu_bl = din("wu_bl", [64, 128, KT * 128])
    wd_bl = din("wd_bl", [4, 16, 128, KT * 128])
    wqt_bl = din("wqt_bl", [NH, 128, KT * 128])
    wkt_bl = din("wkt_bl", [NH, 128, KT * 128])
    wvt_bl = din("wvt_bl", [8, 128, KT * 256])
    wot_bl = din("wot_bl", [16, 128, KT * 128])
    gt_fm = din("gt_fm", [128, KT * 16])
    lnsb = din("lnsb", [128, 2 * KT])
    masks_in = din("masks_in", [2, 128, 6 * 512], BF16)

    out = nc.dram_tensor("out", [D, OUT_TOK], F32, kind="ExternalOutput").ap()
    okind = "ExternalOutput" if BUILD_DEBUG else "Internal"
    xmid = nc.dram_tensor("xmid", [D, TOK], F32, kind=okind).ap()
    x2v = nc.dram_tensor("x2v", [D, OUT_TOK], F32, kind=okind).ap()
    qsp = nc.dram_tensor("qsp", [NH, 128, TOK], BF16, kind=okind).ap()
    vsp = nc.dram_tensor("vsp", [NT, 128, D], BF16, kind=okind).ap()
    atsp = nc.dram_tensor("atsp", [NH, 128, OUT_TOK], BF16, kind=okind).ap()

    sc = Sched()

    with contextlib.ExitStack() as top:
        sems = {s: top.enter_context(nc.semaphore("sem_" + s))
                for s in ALL_SEMS}
        blk = top.enter_context(nc.Block())
        hw = {s: {} for s in STREAMS}

        def flush():
            for stream in STREAMS:
                ops = sc.q[stream]
                if not ops:
                    continue
                sc.q[stream] = []

                def run(eng, ops=ops, stream=stream):
                    h = hw[stream]
                    for waits, fn, sem, n in ops:
                        for s, v in waits.items():
                            if v > h.get(s, 0):
                                eng.wait_ge(sems[s], v)
                                h[s] = v
                        if fn is not None:
                            fn(eng).then_inc(sems[sem], n)

                getattr(blk, {"pe": "tensor", "act": "scalar", "dve": "vector",
                              "pool": "gpsimd", "sync": "sync"}[stream])(run)

        consts = top.enter_context(nc.sbuf_tensor("consts", [128, 4], F32))
        ones_bf = top.enter_context(nc.sbuf_tensor("ones_bf", [128, 1], BF16))
        ones_f1 = top.enter_context(nc.sbuf_tensor("ones_f1", [1, 128], F32))
        ones_f32 = top.enter_context(nc.sbuf_tensor("ones_f32", [128, 1], F32))
        ones_bf1 = top.enter_context(nc.sbuf_tensor("ones_bf1", [1, 128], BF16))
        lnv = top.enter_context(nc.sbuf_tensor("lnv", [128, 2 * KT], F32))
        gtb = top.enter_context(nc.sbuf_tensor("gtb", [128, KT * 16], BF16))
        kg = top.enter_context(nc.sbuf_tensor("kg", [128, NH * 16], BF16))
        vg = top.enter_context(nc.sbuf_tensor("vg", [16, D], BF16))
        ps_all = top.enter_context(nc.psum_tensor("ps_all", [128, 4096], F32))

        PSB = [("ps%d" % b,) for b in range(8)]

        sc.op("dve", _drained(lambda e: e.memset(consts[:, 0:1], EPS)),
              writes=[("c",)])
        sc.op("dve", _drained(lambda e: e.memset(consts[:, 1:2], -SHIFT)),
              writes=[("c",)])
        sc.op("dve", _drained(lambda e: e.memset(ones_bf[:], 1.0)),
              writes=[("c",)])
        sc.op("dve", _drained(lambda e: e.memset(ones_f1[:], 1.0)),
              writes=[("c",)])
        sc.op("dve", _drained(lambda e: e.memset(ones_f32[:], 1.0)),
              writes=[("c",)])
        sc.op("dve", _drained(lambda e: e.memset(ones_bf1[:], 1.0)),
              writes=[("c",)])
        sc.op("sync", lambda e: e.dma_start(out=lnv[:], in_=lnsb),
              writes=[("c",)], dma=True)
        sc.op("pool", lambda e: e.dma_start(out=gtb[:], in_=gt_fm),
              writes=[("c",)], dma=True)
        sc.barrier()

        def stats_rows(tag, xf, sqt, cw, rows, xreads, extra_rho=False,
                       persist=None):
            mu_r, v_r, t_r, r_r = rows
            for kt in range(KT):
                sc.op("act", _drained(lambda e, kt=kt: e.activation(
                    sqt(kt), xf(kt), AF.Square)),
                    reads=xreads(kt), writes=[(tag, "sqt", kt)])
            for kt in range(KT):
                fn = lambda e, kt=kt: e.matmul(
                    ps_all[0:1, 3072:3072 + cw], ones_f32[:], xf(kt),
                    start=(kt == 0), stop=(kt == KT - 1))
                sc.op("pe", _drained(fn) if kt == KT - 1 else fn,
                      reads=xreads(kt),
                      writes=[("ps6",)] + ([(tag, "r1d")] if kt == KT - 1
                                           else []))
            for kt in range(KT):
                fn = lambda e, kt=kt: e.matmul(
                    ps_all[0:1, 3584:3584 + cw], ones_bf[:], sqt(kt),
                    start=(kt == 0), stop=(kt == KT - 1))
                sc.op("pe", _drained(fn) if kt == KT - 1 else fn,
                      reads=[(tag, "sqt", kt)],
                      writes=[("ps7",)] + ([(tag, "r2d")] if kt == KT - 1
                                           else []))
            sc.op("dve", _drained(lambda e: e.tensor_scalar_mul(
                mu_r, ps_all[0:1, 3072:3072 + cw], 1.0 / D)),
                reads=[("ps6",)], writes=[(tag, "mur")])
            sc.op("dve", _drained(lambda e: e.tensor_scalar_mul(
                v_r, ps_all[0:1, 3584:3584 + cw], 1.0 / D)),
                reads=[("ps7",)], writes=[(tag, "vr")])
            sc.op("dve", _drained(lambda e: e.tensor_tensor(t_r, mu_r, mu_r, ALU.mult)),
                  reads=[(tag, "mur")], writes=[(tag, "tr")])
            sc.op("dve", _drained(lambda e: e.tensor_tensor(v_r, v_r, t_r, ALU.subtract)),
                  reads=[(tag, "vr"), (tag, "tr")], writes=[(tag, "vr")])
            sc.op("act", _drained(lambda e: e.activation(
                t_r, v_r, AF.Sqrt, bias=consts[0:1, 0:1])),
                reads=[(tag, "vr"), ("c",)], writes=[(tag, "tr")])
            sc.op("dve", _drained(lambda e: e.reciprocal(r_r, t_r)),
                  reads=[(tag, "tr")], writes=[(tag, "rr")])
            if extra_rho:
                rho_r = v_r
                sc.op("dve", _drained(lambda e: e.tensor_tensor(t_r, r_r, r_r, ALU.mult)),
                      reads=[(tag, "rr")], writes=[(tag, "tr")])
                sc.op("dve", _drained(lambda e: e.tensor_tensor(t_r, t_r, v_r, ALU.mult)),
                      reads=[(tag, "tr"), (tag, "vr")], writes=[(tag, "tr")])
                sc.op("act", _drained(lambda e: e.activation(
                    t_r, t_r, AF.Sqrt, bias=consts[0:1, 0:1])),
                    reads=[(tag, "tr"), ("c",)], writes=[(tag, "tr")])
                if persist is not None:
                    rp, off = persist
                    sc.op("dve", lambda e, rp=rp, off=off: e.tensor_copy(
                        rp[0:1, off:off + cw], mu_r),
                        reads=[(tag, "mur")], writes=[("r2rows",)])
                    sc.op("dve", lambda e, rp=rp, off=off: e.tensor_copy(
                        rp[0:1, TOK + off:TOK + off + cw], r_r),
                        reads=[(tag, "rr")], writes=[("r2rows",)])
                sc.op("dve", _drained(lambda e, rho_r=rho_r: e.reciprocal(rho_r, t_r)),
                      reads=[(tag, "tr")], writes=[(tag, "vr")])
                sc.op("dve", _drained(lambda e, rho_r=rho_r: e.tensor_tensor(
                    rho_r, rho_r, r_r, ALU.mult)),
                    reads=[(tag, "vr"), (tag, "rr")], writes=[(tag, "vr")])
                if persist is not None:
                    rp, off = persist
                    sc.op("dve", lambda e, rp=rp, off=off, rho_r=rho_r:
                          e.tensor_copy(rp[0:1, 2 * TOK + off:
                                           2 * TOK + off + cw], rho_r),
                          reads=[(tag, "vr")], writes=[("r2rows",)])

        def bcast(tag, row, dst, cw, rreg, wreg, bank):
            col = 3072 if bank == 6 else 3584
            sc.op("pe", _drained(lambda e: e.matmul(
                ps_all[:, col:col + cw], ones_f1[:], row,
                start=True, stop=True)),
                reads=[(tag, rreg), ("c",)], writes=[("ps%d" % bank,)])
            sc.op("dve", _drained(lambda e: e.tensor_copy(
                dst, ps_all[:, col:col + cw])),
                reads=[("ps%d" % bank,)], writes=[(tag, wreg)])

        def ffn_pass(tag, src, dst, toff, W, ncv):
            with contextlib.ExitStack() as ph:
                Xb = ph.enter_context(nc.sbuf_tensor(tag + "X", [128, KT * W], F32))
                XLNb = ph.enter_context(nc.sbuf_tensor(tag + "xln", [128, KT * W], BF16))
                HTb = ph.enter_context(nc.sbuf_tensor(tag + "ht", [128, KT * W], BF16))
                NPAR = 3
                par_b = ph.enter_context(nc.sbuf_tensor(tag + "par", [128, NPAR * KT * 128], BF16))
                mu_bc = ph.enter_context(nc.sbuf_tensor(tag + "mub", [128, W], F32))
                r_bc = ph.enter_context(nc.sbuf_tensor(tag + "rb", [128, W], F32))
                rows = ph.enter_context(nc.sbuf_tensor(tag + "rw", [1, 4 * 512], F32))
                tb = ph.enter_context(nc.sbuf_tensor(tag + "tb", [128, W], F32))
                gsb = ph.enter_context(nc.sbuf_tensor(tag + "gs", [128, 2 * 512], BF16))
                HW = W // 2
                sqb = ph.enter_context(nc.sbuf_tensor(tag + "sq", [128, 4 * HW], BF16))
                thb = ph.enter_context(nc.sbuf_tensor(tag + "th", [128, 4 * HW], BF16))

                X = lambda kt: Xb[:, kt * W:(kt + 1) * W]
                XLN = lambda kt: XLNb[:, kt * W:(kt + 1) * W]
                HT = lambda sl: HTb[:, sl * W:(sl + 1) * W]

                for kt in range(KT):
                    sc.op("sync", lambda e, kt=kt: e.dma_start(
                        out=X(kt), in_=src[kt * 128:(kt + 1) * 128,
                                           toff:toff + W]),
                        writes=[(tag, "X", kt)], dma=True)

                off = 0
                for ci, cw in enumerate(ncv):
                    co = off

                    def xf(kt, co=co, cw=cw):
                        return X(kt)[:, co:co + cw]

                    def sqt(kt, co=co, cw=cw):
                        return XLN(kt)[:, co:co + cw]

                    def xreads(kt, co=co):
                        return [(tag, "X", kt)]

                    rws = [rows[0:1, i * 512:i * 512 + cw] for i in range(4)]
                    stats_rows(tag, xf, sqt, cw, rws, xreads)
                    bcast(tag, rws[0], mu_bc[:, co:co + cw], cw, "mur", "mub", 6)
                    bcast(tag, rws[3], r_bc[:, co:co + cw], cw, "rr", "rbb", 7)
                    off += cw
                for kt in range(KT):
                    sc.op("dve", lambda e, kt=kt: e.tensor_tensor(
                        tb[:], X(kt), mu_bc[:], ALU.subtract),
                        reads=[(tag, "X", kt), (tag, "mub")], writes=[(tag, "tb")])
                    sc.op("dve", lambda e, kt=kt: e.tensor_tensor(
                        XLN(kt), tb[:], r_bc[:], ALU.mult),
                        reads=[(tag, "tb"), (tag, "rbb"), (tag, "r2d")],
                        writes=[(tag, "xln")])

                pslot = [0]

                def parcel(dram_ap):
                    s = pslot[0] % NPAR
                    pslot[0] += 1
                    dstp = par_b[:, s * KT * 128:(s + 1) * KT * 128]
                    sc.op("pool", lambda e, dstp=dstp, dram_ap=dram_ap:
                          e.dma_start(out=dstp, in_=dram_ap),
                          writes=[(tag, "par", s)], dma=True)
                    return dstp, (tag, "par", s)

                for bi in range(4):
                    for quad in range(4):
                        for p in range(4):
                            it = bi * 16 + quad * 4 + p
                            slot = quad * 4 + p
                            gpar, gr = parcel(wg_bl[it])
                            upar, ur = parcel(wu_bl[it])
                            off = 0
                            for ci, cw in enumerate(ncv):
                                gbank, ubank = (0, 1) if ci % 2 == 0 else (2, 3)

                                def mmg(e, par=gpar, off=off, cw=cw, bank=gbank):
                                    for k in range(KT):
                                        r = e.matmul(
                                            ps_all[:, bank * 512:bank * 512 + cw],
                                            par[:, k * 128:(k + 1) * 128],
                                            XLN(k)[:, off:off + cw],
                                            start=(k == 0), stop=(k == KT - 1))
                                    return r

                                sc.op("pe", mmg, reads=[gr, (tag, "xln")],
                                      writes=[PSB[gbank]])
                                gs = gsb[:, (ci % 2) * 512:(ci % 2) * 512 + cw]
                                sc.op("act", lambda e, gs=gs, cw=cw, bank=gbank:
                                      e.activation(gs, ps_all[:, bank * 512:bank * 512 + cw],
                                                   AF.Silu),
                                      reads=[PSB[gbank]], writes=[(tag, "gs", ci % 2)])

                                def mmu(e, par=upar, off=off, cw=cw, bank=ubank):
                                    for k in range(KT):
                                        r = e.matmul(
                                            ps_all[:, bank * 512:bank * 512 + cw],
                                            par[:, k * 128:(k + 1) * 128],
                                            XLN(k)[:, off:off + cw],
                                            start=(k == 0), stop=(k == KT - 1))
                                    return r

                                sc.op("pe", mmu, reads=[ur, (tag, "xln")],
                                      writes=[PSB[ubank]])
                                sc.op("dve", lambda e, gs=gs, off=off, cw=cw,
                                      slot=slot, bank=ubank: e.tensor_tensor(
                                          HT(slot)[:, off:off + cw], gs,
                                          ps_all[:, bank * 512:bank * 512 + cw],
                                          ALU.mult),
                                      reads=[(tag, "gs", ci % 2), PSB[ubank]],
                                      writes=[(tag, "ht", slot)])
                                off += cw
                        sl0 = quad * 4
                        hrd = [(tag, "ht", sl0 + p) for p in range(4)]
                        for hf in range(2):
                            o2 = hf * HW
                            SQ = lambda p: sqb[:, p * HW:(p + 1) * HW]
                            TH = lambda i: thb[:, i * HW:(i + 1) * HW]
                            for p in range(4):
                                sc.op("dve", lambda e, p=p, o2=o2, SQ=SQ, sl0=sl0:
                                    e.tensor_tensor(
                                        SQ(p), HT(sl0 + p)[:, o2:o2 + HW],
                                        HT(sl0 + p)[:, o2:o2 + HW], ALU.mult),
                                    reads=[hrd[p]], writes=[(tag, "sqp", p)])
                            sqr = [(tag, "sqp", p) for p in range(4)]
                            for args in [(0, 0, 1, ALU.max, sqr[0:2], "thA"),
                                         (1, 0, 1, ALU.min, sqr[0:2], "thB"),
                                         (2, 2, 3, ALU.max, sqr[2:4], "thC"),
                                         (3, 2, 3, ALU.min, sqr[2:4], "thD")]:
                                i, pa, pb_, op, rd, wr = args
                                sc.op("dve", lambda e, i=i, pa=pa, pb_=pb_, op=op,
                                      SQ=SQ, TH=TH: e.tensor_tensor(
                                          TH(i), SQ(pa), SQ(pb_), op),
                                      reads=rd, writes=[(tag, wr)])
                            sc.op("dve", lambda e, TH=TH: e.tensor_tensor(
                                TH(0), TH(0), TH(2), ALU.min),
                                reads=[(tag, "thA"), (tag, "thC")],
                                writes=[(tag, "thA")])
                            sc.op("dve", lambda e, TH=TH: e.tensor_tensor(
                                TH(1), TH(1), TH(3), ALU.max),
                                reads=[(tag, "thB"), (tag, "thD")],
                                writes=[(tag, "thB")])
                            sc.op("dve", lambda e, TH=TH: e.tensor_tensor(
                                TH(0), TH(0), TH(1), ALU.max),
                                reads=[(tag, "thA"), (tag, "thB")],
                                writes=[(tag, "thA")])
                            for p in range(4):
                                sc.op("dve", lambda e, p=p, SQ=SQ, TH=TH:
                                      e.tensor_tensor(SQ(p), SQ(p), TH(0),
                                                      ALU.is_ge),
                                      reads=[(tag, "sqp", p), (tag, "thA")],
                                      writes=[(tag, "sqp", p)])
                                sc.op("dve", lambda e, p=p, o2=o2, SQ=SQ, sl0=sl0:
                                      e.tensor_tensor(
                                          HT(sl0 + p)[:, o2:o2 + HW],
                                          HT(sl0 + p)[:, o2:o2 + HW], SQ(p),
                                          ALU.mult),
                                      reads=[(tag, "sqp", p), hrd[p]],
                                      writes=[hrd[p]])
                    for dt in range(16):
                        dpar, dr = parcel(wd_bl[bi, dt])
                        off = 0
                        for ci, cw in enumerate(ncv):
                            dbank = 4 + ci % 2

                            def mmd(e, par=dpar, off=off, cw=cw, bank=dbank):
                                for k in range(KT):
                                    r = e.matmul(
                                        ps_all[:, bank * 512:bank * 512 + cw],
                                        par[:, k * 128:(k + 1) * 128],
                                        HT(k)[:, off:off + cw],
                                        start=(k == 0), stop=(k == KT - 1))
                                return r

                            sc.op("pe", mmd,
                                  reads=[dr] + [(tag, "ht", k) for k in range(KT)],
                                  writes=[PSB[dbank]])
                            sc.op("dve", lambda e, dt=dt, off=off, cw=cw,
                                  bank=dbank: e.scalar_tensor_tensor(
                                      X(dt)[:, off:off + cw],
                                      ps_all[:, bank * 512:bank * 512 + cw],
                                      0.5, X(dt)[:, off:off + cw],
                                      ALU.mult, ALU.add),
                                  reads=[PSB[dbank], (tag, "X", dt)],
                                  writes=[(tag, "X", dt)])
                            off += cw

                for kt in range(KT):
                    sc.op("sync", lambda e, kt=kt: e.dma_start(
                        out=dst[kt * 128:(kt + 1) * 128, toff:toff + W],
                        in_=X(kt)),
                        reads=[(tag, "X", kt)], writes=[("dram_dst",)], dma=True)
                sc.barrier()
                flush()

        ffn_pass("p1a", x0c, xmid, 0, 1152, [384, 384, 384])
        ffn_pass("p1b", x0c, xmid, 1152, 1152, [384, 384, 384])

        with contextlib.ExitStack() as ph2o:
            r2rows = ph2o.enter_context(
                nc.sbuf_tensor("r2rows", [1, 3 * TOK], F32))
            with contextlib.ExitStack() as ph2:
                K_all = ph2.enter_context(
                    nc.sbuf_tensor("kall", [128, NH * NT * 128], BF16))

                with contextlib.ExitStack() as phxn:
                    xn_b = phxn.enter_context(
                        nc.sbuf_tensor("xn", [128, KT * TOK], BF16))
                    XN = lambda kt: xn_b[:, kt * TOK:(kt + 1) * TOK]

                    with contextlib.ExitStack() as php:
                        tag = "p2p"
                        CW = 256
                        XCb = php.enter_context(
                            nc.sbuf_tensor("xc", [128, KT * CW], F32))
                        rows = php.enter_context(
                            nc.sbuf_tensor("p2rw", [1, 4 * CW], F32))
                        mu_bc = php.enter_context(
                            nc.sbuf_tensor("p2mb", [128, CW], F32))
                        rho_bc = php.enter_context(
                            nc.sbuf_tensor("p2rb", [128, CW], F32))
                        t2 = php.enter_context(
                            nc.sbuf_tensor("p2t2", [128, CW], F32))
                        XC = lambda kt: XCb[:, kt * CW:(kt + 1) * CW]
                        for ci in range(TOK // CW):
                            co = ci * CW
                            for kt in range(KT):
                                sc.op("sync", lambda e, kt=kt, co=co: e.dma_start(
                                    out=XC(kt),
                                    in_=xmid[kt * 128:(kt + 1) * 128, co:co + CW]),
                                    reads=[("dram_dst",)],
                                    writes=[(tag, "XC", kt)], dma=True)
                            def xf(kt):
                                return XC(kt)

                            def sqt(kt, co=co):
                                return XN(kt)[:, co:co + CW]

                            def xreads(kt):
                                return [(tag, "XC", kt)]

                            rws = [rows[0:1, i * CW:(i + 1) * CW] for i in range(4)]
                            stats_rows(tag, xf, sqt, CW, rws, xreads,
                                       extra_rho=True,
                                       persist=(r2rows[:], co))
                            bcast(tag, rws[0], mu_bc[:], CW, "mur", "mub", 6)
                            bcast(tag, rws[1], rho_bc[:], CW, "vr", "rhob", 7)
                            for kt in range(KT):
                                sc.op("dve", lambda e, kt=kt: e.tensor_tensor(
                                    t2[:], XC(kt), mu_bc[:], ALU.subtract),
                                    reads=[(tag, "XC", kt), (tag, "mub")],
                                    writes=[(tag, "t2")])
                                sc.op("dve", lambda e: e.tensor_tensor(
                                    t2[:], t2[:], rho_bc[:], ALU.mult),
                                    reads=[(tag, "t2"), (tag, "rhob")],
                                    writes=[(tag, "t2")])
                                sc.op("dve", lambda e, kt=kt, co=co: e.tensor_scalar(
                                    XN(kt)[:, co:co + CW], t2[:],
                                    lnv[:, kt:kt + 1], lnv[:, KT + kt:KT + kt + 1],
                                    ALU.mult, ALU.add),
                                    reads=[(tag, "t2"), ("c",), (tag, "r2d")],
                                    writes=[("xn",)])
                        sc.barrier()
                        flush()

                    with contextlib.ExitStack() as phv:
                        tag = "p2v"
                        vparb = phv.enter_context(
                            nc.sbuf_tensor("vpar", [128, 2 * KT * 256], BF16))
                        stg = phv.enter_context(
                            nc.sbuf_tensor("vstg", [128, 2 * 256], BF16))
                        for jc in range(8):
                            sl = jc % 2
                            vpar = vparb[:, sl * KT * 256:(sl + 1) * KT * 256]
                            sc.op("pool", lambda e, vpar=vpar, jc=jc: e.dma_start(
                                out=vpar, in_=wvt_bl[jc]),
                                writes=[(tag, "par", sl)], dma=True)
                            for tile in range(NT):
                                bank = tile % 2

                                def mmv(e, vpar=vpar, tile=tile, bank=bank):
                                    for k in range(KT):
                                        r = e.matmul(
                                            ps_all[:, bank * 512:bank * 512 + 256],
                                            XN(k)[:, tile * 128:(tile + 1) * 128],
                                            vpar[:, k * 256:(k + 1) * 256],
                                            start=(k == 0), stop=(k == KT - 1))
                                    return r

                                sc.op("pe", mmv, reads=[(tag, "par", sl)],
                                      writes=[PSB[bank]])
                                st = stg[:, bank * 256:(bank + 1) * 256]
                                sc.op("act", lambda e, st=st, bank=bank: e.copy(
                                    st, ps_all[:, bank * 512:bank * 512 + 256]),
                                    reads=[PSB[bank]], writes=[(tag, "st", bank)])
                                sc.op("sync", lambda e, st=st, tile=tile, jc=jc:
                                      e.dma_start(
                                          out=vsp[tile, :, jc * 256:(jc + 1) * 256],
                                          in_=st),
                                      reads=[(tag, "st", bank)],
                                      writes=[("vsp",)], dma=True)

                            def mmvg(e, vpar=vpar):
                                for k in range(KT):
                                    r = e.matmul(ps_all[0:16, 1024:1280],
                                             gtb[:, k * 16:(k + 1) * 16],
                                             vpar[:, k * 256:(k + 1) * 256],
                                             start=(k == 0), stop=(k == KT - 1))
                                return r

                            sc.op("pe", mmvg, reads=[(tag, "par", sl), ("c",)],
                                  writes=[PSB[2]])
                            sc.op("act", lambda e, jc=jc: e.copy(
                                vg[:, jc * 256:(jc + 1) * 256],
                                ps_all[0:16, 1024:1280]),
                                reads=[PSB[2]], writes=[("vg",)])
                        sc.barrier()
                        flush()

                    with contextlib.ExitStack() as phk:
                        tag = "p2k"
                        kparb = phk.enter_context(
                            nc.sbuf_tensor("kpar", [128, 2 * KT * 128], BF16))
                        stg = phk.enter_context(
                            nc.sbuf_tensor("kstg", [128, 2 * 512], BF16))
                        for h in range(NH):
                            sl = h % 2
                            kpar = kparb[:, sl * KT * 128:(sl + 1) * KT * 128]
                            sc.op("pool", lambda e, kpar=kpar, h=h: e.dma_start(
                                out=kpar, in_=wkt_bl[h]),
                                writes=[(tag, "par", sl)], dma=True)
                            for ci in range(6):
                                co = ci * 384
                                bank = ci % 2

                                def mmk(e, kpar=kpar, co=co, bank=bank):
                                    for k in range(KT):
                                        r = e.matmul(
                                            ps_all[:, bank * 512:bank * 512 + 384],
                                            kpar[:, k * 128:(k + 1) * 128],
                                            XN(k)[:, co:co + 384],
                                            start=(k == 0), stop=(k == KT - 1))
                                    return r

                                sc.op("pe", mmk, reads=[(tag, "par", sl)],
                                      writes=[PSB[bank]])
                                sc.op("act", lambda e, h=h, co=co, bank=bank: e.copy(
                                    K_all[:, h * NT * 128 + co:
                                          h * NT * 128 + co + 384],
                                    ps_all[:, bank * 512:bank * 512 + 384]),
                                    reads=[PSB[bank]], writes=[("kall", h)])

                            def mmkg(e, kpar=kpar):
                                for k in range(KT):
                                    r = e.matmul(ps_all[:, 1024:1040],
                                             kpar[:, k * 128:(k + 1) * 128],
                                             gtb[:, k * 16:(k + 1) * 16],
                                             start=(k == 0), stop=(k == KT - 1))
                                return r

                            sc.op("pe", mmkg, reads=[(tag, "par", sl), ("c",)],
                                  writes=[PSB[2]])
                            sc.op("act", lambda e, h=h: e.copy(
                                kg[:, h * 16:(h + 1) * 16], ps_all[:, 1024:1040]),
                                reads=[PSB[2]], writes=[("kg",)])
                        tag = "p2q"
                        for h in range(NH):
                            sl = h % 2
                            qpar = kparb[:, sl * KT * 128:(sl + 1) * KT * 128]
                            sc.op("pool", lambda e, qpar=qpar, h=h: e.dma_start(
                                out=qpar, in_=wqt_bl[h]),
                                writes=[("p2k", "par", sl)], dma=True)
                            for ci in range(6):
                                co = ci * 384
                                bank = ci % 2

                                def mmq(e, qpar=qpar, co=co, bank=bank):
                                    for k in range(KT):
                                        r = e.matmul(
                                            ps_all[:, bank * 512:bank * 512 + 384],
                                            qpar[:, k * 128:(k + 1) * 128],
                                            XN(k)[:, co:co + 384],
                                            start=(k == 0), stop=(k == KT - 1))
                                    return r

                                sc.op("pe", mmq, reads=[("p2k", "par", sl)],
                                      writes=[PSB[bank]])
                                st = stg[:, bank * 512:bank * 512 + 384]
                                sc.op("act", lambda e, st=st, bank=bank: e.copy(
                                    st, ps_all[:, bank * 512:bank * 512 + 384]),
                                    reads=[PSB[bank]], writes=[(tag, "st", bank)])
                                sc.op("sync", lambda e, st=st, h=h, co=co:
                                      e.dma_start(out=qsp[h, :, co:co + 384],
                                                  in_=st),
                                      reads=[(tag, "st", bank)],
                                      writes=[("qsp",)], dma=True)
                        sc.barrier()
                        flush()

                with contextlib.ExitStack() as phb:
                    tag = "p2b"
                    Qb = phb.enter_context(
                        nc.sbuf_tensor("qb", [128, NH * 512], BF16))
                    Vb = phb.enter_context(
                        nc.sbuf_tensor("vb", [128, 6 * D], BF16))
                    PT3 = phb.enter_context(
                        nc.sbuf_tensor("pt", [128, 3 * 3584], BF16))
                    maskb = phb.enter_context(
                        nc.sbuf_tensor("maskb", [128, 3072], BF16))
                    ATb = phb.enter_context(
                        nc.sbuf_tensor("atb", [128, NH * 512], BF16))
                    rbuf = phb.enter_context(
                        nc.sbuf_tensor("rbuf", [128, 2 * 512], F32))
                    tbn = phb.enter_context(
                        nc.sbuf_tensor("tbn", [128, 512], F32))
                    srow = phb.enter_context(
                        nc.sbuf_tensor("srow", [1, 2 * 512], BF16))

                    def head_front(qc, h):
                        p3 = h % 3
                        p2 = h % 2
                        PT = PT3[:, p3 * 3584:(p3 + 1) * 3584]
                        gc = 3072 + p2 * 512

                        def mms(e, h=h, qc=qc):
                            for s in range(6):
                                tile = 4 * qc + s
                                r = e.matmul(ps_all[:, s * 512:(s + 1) * 512],
                                         K_all[:, (h * NT + tile) * 128:
                                               (h * NT + tile + 1) * 128],
                                         Qb[:, h * 512:(h + 1) * 512],
                                         start=True, stop=True)
                            return r

                        sc.op("pe", mms,
                              reads=[("kall", h), (tag, "Q")],
                              writes=[PSB[b] for b in range(6)])
                        sc.op("pe", lambda e, h=h, gc=gc: e.matmul(
                            ps_all[0:16, gc:gc + 512],
                            kg[:, h * 16:(h + 1) * 16],
                            Qb[:, h * 512:(h + 1) * 512],
                            start=True, stop=True),
                            reads=[("kg",), (tag, "Q")], writes=[PSB[6 + p2]])
                        sc.op("act", lambda e, PT=PT: e.activation(
                            PT[:, 0:3072], ps_all[:, 0:3072], AF.Exp,
                            bias=consts[:, 1:2], scale=SCALE),
                            reads=[PSB[b] for b in range(6)] + [("c",)],
                            writes=[(tag, "ptw", p3)])
                        sc.op("act", lambda e, PT=PT, gc=gc: e.activation(
                            PT[0:16, 3072:3584], ps_all[0:16, gc:gc + 512],
                            AF.Exp, bias=consts[0:16, 1:2], scale=SCALE),
                            reads=[PSB[6 + p2], ("c",)],
                            writes=[(tag, "ptg", p3)])
                        sc.op("dve", _drained(lambda e, PT=PT: e.tensor_tensor(
                            PT[:, 0:3072], PT[:, 0:3072], maskb[:],
                            ALU.mult)),
                            reads=[(tag, "ptw", p3), ("mask",)],
                            writes=[(tag, "ptw", p3)])

                    def head_tail1(qc, h):
                        p3 = h % 3
                        p2 = h % 2
                        PT = PT3[:, p3 * 3584:(p3 + 1) * 3584]
                        gc = 3072 + p2 * 512

                        def mmsum(e, PT=PT, gc=gc):
                            for s in range(6):
                                e.matmul(ps_all[0:1, gc:gc + 512],
                                         ones_bf[:],
                                         PT[:, s * 512:(s + 1) * 512],
                                         start=(s == 0), stop=False)
                            return e.matmul(ps_all[0:1, gc:gc + 512],
                                            ones_bf[0:16, :],
                                            PT[0:16, 3072:3584],
                                            start=False, stop=True)

                        sc.op("pe", mmsum,
                              reads=[(tag, "ptw", p3), (tag, "ptg", p3), ("c",)],
                              writes=[PSB[6 + p2]])
                        sc.op("dve", _drained(lambda e, gc=gc, p2=p2: e.tensor_copy(
                            srow[0:1, p2 * 512:(p2 + 1) * 512],
                            ps_all[0:1, gc:gc + 512])),
                            reads=[PSB[6 + p2]], writes=[("srow", p2)])

                    def head_tail2(qc, h):
                        p3 = h % 3
                        p2 = h % 2
                        PT = PT3[:, p3 * 3584:(p3 + 1) * 3584]
                        gc = 3072 + p2 * 512

                        sc.op("pe", lambda e, gc=gc, p2=p2: e.matmul(
                            ps_all[:, gc:gc + 512], ones_bf1[:],
                            srow[0:1, p2 * 512:(p2 + 1) * 512],
                            start=True, stop=True),
                            reads=[("srow", p2), ("c",)], writes=[PSB[6 + p2]])
                        sc.op("act", _drained(lambda e, gc=gc: e.activation(
                            tbn[:], ps_all[:, gc:gc + 512], AF.Ln)),
                            reads=[PSB[6 + p2]], writes=[("tbn",)])
                        sc.op("act", lambda e, p2=p2: e.activation(
                            rbuf[:, p2 * 512:(p2 + 1) * 512], tbn[:],
                            AF.Exp, scale=-1.0),
                            reads=[("tbn",)], writes=[("rbc", p2)])

                        def mmav(e, qc=qc, h=h, PT=PT, gc=gc):
                            for s in range(6):
                                vs = (4 * qc + s) % 6
                                e.matmul(ps_all[:, gc:gc + 512],
                                         Vb[:, vs * D + h * 128:
                                            vs * D + (h + 1) * 128],
                                         PT[:, s * 512:(s + 1) * 512],
                                         start=(s == 0), stop=False)
                            return e.matmul(ps_all[:, gc:gc + 512],
                                            vg[:, h * 128:(h + 1) * 128],
                                            PT[0:16, 3072:3584],
                                            start=False, stop=True)

                        sc.op("pe", mmav,
                              reads=[(tag, "ptw", p3), (tag, "ptg", p3),
                                     ("vg",)] +
                                    [(tag, "V", (4 * qc + s) % 6)
                                     for s in range(6)],
                              writes=[PSB[6 + p2]])
                        sc.op("dve", lambda e, h=h, gc=gc, p2=p2:
                              e.tensor_tensor(
                                  ATb[:, h * 512:(h + 1) * 512],
                                  ps_all[:, gc:gc + 512],
                                  rbuf[:, p2 * 512:(p2 + 1) * 512], ALU.mult),
                              reads=[PSB[6 + p2], ("rbc", p2)],
                              writes=[(tag, "at", h)])

                    for qc in range(4):
                        if qc <= 1:
                            sc.op("sync", lambda e, qc=qc: e.dma_start(
                                out=maskb[:], in_=masks_in[qc]),
                                writes=[("mask",)], dma=True)
                        for h in range(NH):
                            sc.op("sync", lambda e, h=h, qc=qc: e.dma_start(
                                out=Qb[:, h * 512:(h + 1) * 512],
                                in_=qsp[h, :, 256 + qc * 512:
                                        256 + (qc + 1) * 512]),
                                reads=[("qsp",)], writes=[(tag, "Q")], dma=True)
                        for s in range(6):
                            tile = 4 * qc + s
                            vs = tile % 6
                            if qc > 0 and s < 2:
                                continue
                            sc.op("sync", lambda e, tile=tile, vs=vs: e.dma_start(
                                out=Vb[:, vs * D:(vs + 1) * D], in_=vsp[tile]),
                                reads=[("vsp",)], writes=[(tag, "V", vs)],
                                dma=True)
                        for h in range(NH):
                            if h >= 2:
                                head_tail2(qc, h - 2)
                            head_front(qc, h)
                            if h >= 1:
                                head_tail1(qc, h - 1)
                        head_tail1(qc, NH - 1)
                        head_tail2(qc, NH - 2)
                        head_tail2(qc, NH - 1)
                        for h in range(NH):
                            sc.op("sync", lambda e, h=h, qc=qc: e.dma_start(
                                out=atsp[h, :, qc * 512:(qc + 1) * 512],
                                in_=ATb[:, h * 512:(h + 1) * 512]),
                                reads=[(tag, "at", h)], writes=[("atsp",)],
                                dma=True)
                    sc.barrier()
                    flush()

            with contextlib.ExitStack() as phc:
                tag = "p2c"
                ATL = phc.enter_context(
                    nc.sbuf_tensor("atl", [128, NH * OUT_TOK], BF16))
                parb = phc.enter_context(
                    nc.sbuf_tensor("p2cpar", [128, 2 * KT * 128], BF16))
                XB = phc.enter_context(
                    nc.sbuf_tensor("p2cxb", [128, OUT_TOK], F32))
                TL = phc.enter_context(
                    nc.sbuf_tensor("p2ctl", [128, OUT_TOK], F32))
                MU2 = phc.enter_context(
                    nc.sbuf_tensor("p2cmu", [128, OUT_TOK], F32))
                R2 = phc.enter_context(
                    nc.sbuf_tensor("p2cr2", [128, OUT_TOK], F32))
                for h in range(NH):
                    sc.op("sync", lambda e, h=h: e.dma_start(
                        out=ATL[:, h * OUT_TOK:(h + 1) * OUT_TOK], in_=atsp[h]),
                        reads=[("atsp",)], writes=[(tag, "atl", h)], dma=True)
                for ci in range(4):
                    co = ci * 512
                    sc.op("pe", lambda e, co=co: e.matmul(
                        ps_all[:, 3072:3584], ones_f1[:],
                        r2rows[0:1, 256 + co:256 + co + 512],
                        start=True, stop=True),
                        reads=[("r2rows",), ("c",)], writes=[PSB[6]])
                    sc.op("dve", lambda e, co=co: e.tensor_copy(
                        MU2[:, co:co + 512], ps_all[:, 3072:3584]),
                        reads=[PSB[6]], writes=[(tag, "mu2")])
                    sc.op("pe", lambda e, co=co: e.matmul(
                        ps_all[:, 3584:4096], ones_f1[:],
                        r2rows[0:1, TOK + 256 + co:TOK + 256 + co + 512],
                        start=True, stop=True),
                        reads=[("r2rows",), ("c",)], writes=[PSB[7]])
                    sc.op("dve", lambda e, co=co: e.tensor_copy(
                        R2[:, co:co + 512], ps_all[:, 3584:4096]),
                        reads=[PSB[7]], writes=[(tag, "r2")])
                for dt in range(KT):
                    sl = dt % 2
                    par = parb[:, sl * KT * 128:(sl + 1) * KT * 128]
                    sc.op("pool", lambda e, par=par, dt=dt: e.dma_start(
                        out=par, in_=wot_bl[dt]),
                        writes=[(tag, "par", sl)], dma=True)
                    sc.op("sync", lambda e, dt=dt: e.dma_start(
                        out=XB[:], in_=xmid[dt * 128:(dt + 1) * 128, 256:TOK]),
                        reads=[("dram_dst",)], writes=[(tag, "xb")], dma=True)
                    sc.op("dve", lambda e: e.tensor_tensor(
                        TL[:], XB[:], MU2[:], ALU.subtract),
                        reads=[(tag, "xb"), (tag, "mu2")], writes=[(tag, "tl")])
                    sc.op("dve", lambda e: e.tensor_tensor(
                        TL[:], TL[:], R2[:], ALU.mult),
                        reads=[(tag, "tl"), (tag, "r2")], writes=[(tag, "tl")])
                    sc.op("pool", lambda e: e.tensor_tensor(
                        XB[:], XB[:], TL[:], ALU.add),
                        reads=[(tag, "xb"), (tag, "tl")], writes=[(tag, "xb")])
                    for qc in range(4):
                        bank = qc % 2

                        def mmo(e, par=par, qc=qc, bank=bank):
                            for k in range(KT):
                                r = e.matmul(
                                    ps_all[:, bank * 512:(bank + 1) * 512],
                                    par[:, k * 128:(k + 1) * 128],
                                    ATL[:, k * OUT_TOK + qc * 512:
                                        k * OUT_TOK + (qc + 1) * 512],
                                    start=(k == 0), stop=(k == KT - 1))
                            return r

                        sc.op("pe", mmo,
                              reads=[(tag, "par", sl)] +
                                    [(tag, "atl", k) for k in range(KT)],
                              writes=[PSB[bank]])
                        sc.op("dve", lambda e, qc=qc, bank=bank:
                              e.scalar_tensor_tensor(
                                  XB[:, qc * 512:(qc + 1) * 512],
                                  ps_all[:, bank * 512:(bank + 1) * 512], 1.0,
                                  XB[:, qc * 512:(qc + 1) * 512],
                                  ALU.mult, ALU.add),
                              reads=[PSB[bank], (tag, "xb")],
                              writes=[(tag, "xb")])
                    sc.op("sync", lambda e, dt=dt: e.dma_start(
                        out=x2v[dt * 128:(dt + 1) * 128, :], in_=XB[:]),
                        reads=[(tag, "xb")], writes=[("dram_dst",)], dma=True)
                sc.barrier()
                flush()

        ffn_pass("p3a", x2v, out, 0, 1024, [512, 512])
        ffn_pass("p3b", x2v, out, 1024, 1024, [512, 512])
    return nc


_PERM = np.array([4 * g + p for s in range(16) for p in range(4)
                  for g in range(s * 128, (s + 1) * 128)], np.int64)


def _mask_set(first_valid):
    m = np.zeros((6, 128, 512), np.float32)
    for s in range(6):
        ki = s * 128 + np.arange(128)[:, None]
        j = np.arange(512)[None, :]
        qi = (2 + j // 128) * 128 + (j % 128)
        valid = (ki <= qi) & (ki >= qi - 255)
        if first_valid > 0:
            valid = valid & (ki >= first_valid * 128)
        m[s] = valid
    return m


def _mask_input(even_core):
    interior = _mask_set(0)
    first = _mask_set(2) if even_core else interior
    outm = np.zeros((2, 128, 6 * 512), np.float32)
    for i, mm in enumerate([first, interior]):
        for s in range(6):
            outm[i, :, s * 512:(s + 1) * 512] = mm[s]
    return outm.astype(ml_dtypes.bfloat16)


def kernel(L0_attn_x, L0_attn_global_tokens, L0_attn_ln_scale, L0_attn_ln_bias,
           L0_attn_Wq, L0_attn_Wk, L0_attn_Wv, L0_attn_Wo,
           L0_ffn_w_gate, L0_ffn_w_up, L0_ffn_w_down):
    x = np.ascontiguousarray(np.asarray(L0_attn_x, np.float32))
    gt = np.asarray(L0_attn_global_tokens, np.float32)[0]
    lns = np.asarray(L0_attn_ln_scale, np.float32)
    lnb = np.asarray(L0_attn_ln_bias, np.float32)
    Wq = np.asarray(L0_attn_Wq, np.float32)
    Wk = np.asarray(L0_attn_Wk, np.float32)
    Wv = np.asarray(L0_attn_Wv, np.float32)
    Wo = np.asarray(L0_attn_Wo, np.float32)
    wg = np.asarray(L0_ffn_w_gate, np.float32)
    wu = np.asarray(L0_ffn_w_up, np.float32)
    wd = np.asarray(L0_ffn_w_down, np.float32)

    wgp = wg[:, _PERM]
    wup = wu[:, _PERM]
    wdp = wd[_PERM, :]
    wg_bl = np.ascontiguousarray(
        wgp.reshape(16, 128, 64, 128).transpose(2, 1, 0, 3)).reshape(64, 128, 2048)
    wu_bl = np.ascontiguousarray(
        wup.reshape(16, 128, 64, 128).transpose(2, 1, 0, 3)).reshape(64, 128, 2048)
    wd_bl = np.ascontiguousarray(
        wdp.reshape(4, 16, 128, 16, 128).transpose(0, 3, 2, 1, 4)).reshape(4, 16, 128, 2048)

    def jt_block(WT):
        return np.ascontiguousarray(
            WT.reshape(16, 128, 16, 128).transpose(2, 1, 0, 3)).reshape(16, 128, 2048)

    wqt_bl = jt_block(np.ascontiguousarray(Wq.T))
    wkt_bl = jt_block(np.ascontiguousarray(Wk.T))
    wot_bl = jt_block(np.ascontiguousarray(Wo.T))
    wvt_bl = np.ascontiguousarray(
        np.ascontiguousarray(Wv.T).reshape(16, 128, 8, 256)
        .transpose(2, 1, 0, 3)).reshape(8, 128, 4096)
    gt_fm = np.ascontiguousarray(
        np.ascontiguousarray(gt.T).reshape(16, 128, 16)
        .transpose(1, 0, 2)).reshape(128, 256)
    lnsb = np.ascontiguousarray(np.concatenate(
        [lns.reshape(16, 128).T, lnb.reshape(16, 128).T], axis=1))

    m_even = _mask_input(True)
    m_odd = _mask_input(False)
    shared = dict(wg_bl=wg_bl, wu_bl=wu_bl, wd_bl=wd_bl, wqt_bl=wqt_bl,
                  wkt_bl=wkt_bl, wvt_bl=wvt_bl, wot_bl=wot_bl, gt_fm=gt_fm,
                  lnsb=lnsb)
    in_maps = []
    for c in range(8):
        b, hf = c // 2, c % 2
        if hf == 0:
            shard = np.concatenate(
                [np.zeros((256, D), np.float32), x[b, :2048]], axis=0)
        else:
            shard = x[b, 2048 - 256:4096]
        m = dict(shared)
        m["x0c"] = np.ascontiguousarray(shard.T)
        m["masks_in"] = m_even if hf == 0 else m_odd
        in_maps.append(m)

    nc = _build()
    res = run_bass_kernel_spmd(nc, in_maps, core_ids=list(range(8)))
    outp = np.empty((4, 4096, D), np.float32)
    for c in range(8):
        b, hf = c // 2, c % 2
        outp[b, hf * 2048:(hf + 1) * 2048, :] = \
            np.asarray(res.results[c]["out"], np.float32).T
    return outp
